# version 12
# baseline (speedup 1.0000x reference)
import numpy as np
import concourse.bass as bass
import concourse.mybir as mybir
from concourse.bass_utils import run_bass_kernel_spmd

F32 = mybir.dt.float32
F32R = mybir.dt.float32r
AX = mybir.AxisListType
ALU = mybir.AluOpType
ACT = mybir.ActivationFunctionType

D = 2048
NCH = 16
NH = 16
INTER = 8192
NIC = 64
SCALE = float(128) ** -0.5
EPS = 1e-5
NEG = -3.0e38

ENGS = ("sync", "tensor", "vector", "scalar", "gpsimd")
DEBUG_DUMP = False

VIS = [[0, 2], [0, 1, 2, 3]]
DIAG_CI = [0, 1]
WIDTH = [1024, 2048]


class Res:
    __slots__ = ("w", "r", "psum")

    def __init__(self, psum=False):
        self.w = None
        self.r = {}
        self.psum = psum


class KB:
    def __init__(self, nc):
        self.nc = nc
        self.stream = {e: [] for e in ENGS}
        self.cnt = {}
        self.waited = {e: {} for e in ENGS}
        self.engsem = {e: "e_" + e for e in ENGS}
        self.marks = []

    def _bump(self, sem, n):
        self.cnt[sem] = self.cnt.get(sem, 0) + n
        return self.cnt[sem]

    def _wait(self, eng, tok):
        sem, val = tok[0], tok[1]
        if self.waited[eng].get(sem, 0) >= val:
            return
        self.waited[eng][sem] = val
        self.stream[eng].append(("w", sem, val))

    def op(self, eng, fn, reads=(), writes=(), dma=None):
        op_psum = any(r.psum for r in tuple(reads) + tuple(writes))
        want = {}
        for r in reads:
            if r.w is not None:
                tok = r.w
                if eng == "tensor" and tok[2] == "tensor":
                    continue
                if want.get(tok[0], -1) < tok[1]:
                    want[tok[0]] = tok[1]
        for w in writes:
            toks = ([w.w] if w.w is not None else []) + list(w.r.values())
            for tok in toks:
                if dma is not None and tok[0] == dma and tok[2] == eng:
                    continue
                if eng == "tensor" and tok[2] == "tensor":
                    continue
                if want.get(tok[0], -1) < tok[1]:
                    want[tok[0]] = tok[1]
        for sem, val in want.items():
            self._wait(eng, (sem, val, None, False))
        if dma is not None:
            prev = self.cnt.get(dma, 0)
            if prev:
                self._wait(eng, (dma, prev, eng, False))
            val = self._bump(dma, 16)
            tok = (dma, val, eng, op_psum)
            self.stream[eng].append(("o", fn, dma, 16))
        else:
            sem = self.engsem[eng]
            val = self._bump(sem, 1)
            tok = (sem, val, eng, op_psum)
            self.stream[eng].append(("o", fn, sem, 1))
        for r in reads:
            old = r.r.get(tok[0])
            if old is None or old[1] < tok[1]:
                r.r[tok[0]] = tok
        for w in writes:
            w.w = tok
            w.r = {}
        return tok

    def mark(self, name):
        idx = len(self.marks)
        self.marks.append((name, {e: len(self.stream[e]) for e in ENGS},
                           dict(self.cnt)))
        self.stream["tensor"].append(("r", lambda E, i=idx: E.notification(i)))

    def barrier_wait(self, eng, sem, val):
        self._wait(eng, (sem, val, "x", False))

    def realize(self, sems_needed):
        nc = self.nc
        from contextlib import ExitStack
        ctx = ExitStack()
        sems = {n: ctx.enter_context(nc.semaphore(n)) for n in sorted(sems_needed)}
        block = ctx.enter_context(nc.Block())

        def runner(engname):
            def _run(e):
                for item in self.stream[engname]:
                    if item[0] == "w":
                        e.wait_ge(sems[item[1]], item[2])
                    elif item[0] == "r":
                        item[1](e)
                    else:
                        item[1](e).then_inc(sems[item[2]], item[3])
            return _run

        for engname in ENGS:
            getattr(block, engname)(runner(engname))
        return ctx


def build_program():
    nc = bass.Bass("TRN2", target_bir_lowering=False)

    xin = nc.dram_tensor("x", [2048, D], F32, kind="ExternalInput").ap()
    wq = nc.dram_tensor("wq", [D, D], F32, kind="ExternalInput").ap()
    wkv = nc.dram_tensor("wkv", [D, 256], F32, kind="ExternalInput").ap()
    wo = nc.dram_tensor("wo", [D, D], F32, kind="ExternalInput").ap()
    wfc = nc.dram_tensor("wfc", [D, INTER], F32, kind="ExternalInput").ap()
    wpr = nc.dram_tensor("wpr", [INTER, D], F32, kind="ExternalInput").ap()
    wfn = nc.dram_tensor("wfn", [D, D], F32, kind="ExternalInput").ap()
    g1i = nc.dram_tensor("g1", [128, NCH], F32, kind="ExternalInput").ap()
    b1i = nc.dram_tensor("b1", [128, NCH], F32, kind="ExternalInput").ap()
    g2i = nc.dram_tensor("g2", [128, NCH], F32, kind="ExternalInput").ap()
    b2i = nc.dram_tensor("b2", [128, NCH], F32, kind="ExternalInput").ap()
    bqi = nc.dram_tensor("bq", [128, NCH], F32, kind="ExternalInput").ap()
    bkvi = nc.dram_tensor("bkv", [128, 2], F32, kind="ExternalInput").ap()
    boi = nc.dram_tensor("bo", [128, NCH], F32, kind="ExternalInput").ap()
    bfci = nc.dram_tensor("bfc", [128, NIC], F32, kind="ExternalInput").ap()
    bpri = nc.dram_tensor("bpr", [128, NCH], F32, kind="ExternalInput").ap()
    bfni = nc.dram_tensor("bfn", [128, NCH], F32, kind="ExternalInput").ap()
    identi = nc.dram_tensor("ident", [128, 128], F32, kind="ExternalInput").ap()
    masksi = nc.dram_tensor("masks", [128, 896], F32, kind="ExternalInput").ap()
    cmaski = nc.dram_tensor("cmask", [128, 8], F32, kind="ExternalInput").ap()
    out = nc.dram_tensor("out", [1024, D], F32, kind="ExternalOutput").ap()
    xfm = nc.dram_tensor("xfm", [D, 1024], F32).ap()
    tfm = nc.dram_tensor("tfm", [D, 1024], F32).ap()
    wqt = nc.dram_tensor("wqt", [D, D], F32).ap()
    acr = nc.dram_tensor("acr", [4, 512], F32).ap()
    dbg = nc.dram_tensor("dbg", [384, 8192], F32).ap() if DEBUG_DUMP else None
    zfm = nc.dram_tensor("zfm", [D, 1024], F32).ap()
    afm = nc.dram_tensor("afm", [D, 1024], F32).ap()

    from contextlib import ExitStack
    outer = ExitStack()
    e = outer.enter_context

    T1 = e(nc.sbuf_tensor("T1", [128, 8192], F32))
    T2 = e(nc.sbuf_tensor("T2", [128, 8192], F32))
    T3 = e(nc.sbuf_tensor("T3", [128, 8192], F32))
    T4 = e(nc.sbuf_tensor("T4", [128, 8192], F32))
    WP = e(nc.sbuf_tensor("WP", [128, 4 * 2048], F32))
    T1R = nc.virtual_sbuf_tensor("T1R", [128, 8192], F32R)
    T2R = nc.virtual_sbuf_tensor("T2R", [128, 8192], F32R)
    T3R = nc.virtual_sbuf_tensor("T3R", [128, 8192], F32R)
    T4R = nc.virtual_sbuf_tensor("T4R", [128, 8192], F32R)
    WPR = nc.virtual_sbuf_tensor("WPR", [128, 4 * 2048], F32R)
    nc.back_virtual_tensor_with(T1R, T1)
    nc.back_virtual_tensor_with(T2R, T2)
    nc.back_virtual_tensor_with(T3R, T3)
    nc.back_virtual_tensor_with(T4R, T4)
    nc.back_virtual_tensor_with(WPR, WP)
    RTW = {id(T1): T1R, id(T2): T2R, id(T3): T3R, id(T4): T4R, id(WP): WPR}
    kbuf = e(nc.sbuf_tensor("kbuf", [128, 2048], F32))
    vt = e(nc.sbuf_tensor("vt", [128, 2048], F32))
    S0 = e(nc.sbuf_tensor("S0", [128, 512], F32))
    S1 = e(nc.sbuf_tensor("S1", [128, 512], F32))
    S2 = e(nc.sbuf_tensor("S2", [128, 512], F32))
    S3 = e(nc.sbuf_tensor("S3", [128, 512], F32))
    bb0 = e(nc.sbuf_tensor("bb0", [128, 512], F32))
    bb1 = e(nc.sbuf_tensor("bb1", [128, 512], F32))
    R = e(nc.sbuf_tensor("R", [1, 2560], F32))
    onec = e(nc.sbuf_tensor("onec", [128, 1], F32))
    oner = e(nc.sbuf_tensor("oner", [1, 128], F32))
    negm = e(nc.sbuf_tensor("negm", [128, 2], F32))
    lsum = e(nc.sbuf_tensor("lsum", [128, 2], F32))
    lrec = e(nc.sbuf_tensor("lrec", [128, 2], F32))
    g1t = e(nc.sbuf_tensor("g1t", [128, NCH], F32))
    b1t = e(nc.sbuf_tensor("b1t", [128, NCH], F32))
    g2t = e(nc.sbuf_tensor("g2t", [128, NCH], F32))
    b2t = e(nc.sbuf_tensor("b2t", [128, NCH], F32))
    bqt = e(nc.sbuf_tensor("bqt", [128, NCH], F32))
    bkvt = e(nc.sbuf_tensor("bkvt", [128, 2], F32))
    bot = e(nc.sbuf_tensor("bot", [128, NCH], F32))
    bfct = e(nc.sbuf_tensor("bfct", [128, NIC], F32))
    bprt = e(nc.sbuf_tensor("bprt", [128, NCH], F32))
    bfnt = e(nc.sbuf_tensor("bfnt", [128, NCH], F32))
    identt = e(nc.sbuf_tensor("identt", [128, 128], F32))
    maskst = e(nc.sbuf_tensor("maskst", [128, 896], F32))
    bb2 = maskst
    absc = bb0
    cmaskt = e(nc.sbuf_tensor("cmaskt", [128, 8], F32))
    sicol = e(nc.sbuf_tensor("sicol", [128, 2], F32))

    pb = [e(nc.psum_tensor(f"pb{i}", [128, 512], F32)) for i in range(8)]

    kb = KB(nc)

    def slices(n):
        return [Res() for _ in range(n)]

    rT1, rT2, rT3, rT4 = slices(NCH), slices(NCH), slices(NCH), slices(NCH)
    rWP = slices(4)
    rK = slices(4)
    rV = slices(4)
    rPT = slices(2)
    rS = slices(4)
    rBB = slices(3)
    rRow = Res()
    rAcr = Res()
    rNegm = [Res(), Res()]
    rLsum = [Res(), Res()]
    rLrec = [Res(), Res()]
    rB = [Res(psum=True) for _ in range(8)]
    rConst = Res()
    rAbs = Res()
    rSic = Res()
    rXfm = [Res(), Res()]
    rTfm = [Res(), Res()]
    rWqt = Res()
    rOnes = Res()

    Stile = [S0, S1, S2, S3]

    def dma(eng, sem, outap, inap, reads, writes):
        return kb.op(eng, lambda E, o=outap, i=inap: E.dma_start(out=o, in_=i),
                     reads=reads, writes=writes, dma=sem)

    def flush(sems, dst_res, clean_res=()):
        for s in sems:
            kb.barrier_wait("gpsimd", s, kb.cnt.get(s, 0))
        tok = kb.op("gpsimd", lambda E: E.engine_nop(), writes=tuple(dst_res))
        for r in clean_res:
            r.r = {s: t for s, t in r.r.items() if not s.startswith("d_")}
            r.r[tok[0]] = tok
        return tok

    def mm(outap, lhs, rhs, bank, start, stop, reads):
        return kb.op("tensor",
                     lambda E, o=outap, l=lhs, r=rhs, st=start, sp=stop:
                     E.matmul(o, l, r, start=st, stop=sp),
                     reads=tuple(reads), writes=(rB[bank],))

    def prewait(eng, reslist):
        want = {}
        for r in reslist:
            if r.w is not None:
                s, v = r.w[0], r.w[1]
                if want.get(s, -1) < v:
                    want[s] = v
        for s, v in want.items():
            kb._wait(eng, (s, v, None, False))

    def colblock(w, col0):
        ld = int(w.shape[-1])
        return bass.AP(w.tensor, col0, [[ld, 128], [128 * ld, 16], [1, 128]])

    def c16(t, c, width=512):
        return t[:, c * width:(c + 1) * width]

    init_loads = [
        (g1t, g1i), (b1t, b1i), (g2t, g2i), (b2t, b2i), (bqt, bqi),
        (bkvt, bkvi), (bot, boi), (bfct, bfci), (bprt, bpri), (bfnt, bfni),
        (identt, identi), (cmaskt, cmaski),
    ]
    for dst, src_ in init_loads:
        dma("sync", "d_init", dst[:], src_, (), (rConst,))
    kb.op("gpsimd", lambda E: E.memset(onec[:], 1.0), writes=(rOnes,))
    kb.op("gpsimd", lambda E: E.memset(oner[:], 1.0), writes=(rOnes,))

    A_, B_, C_, D_, E_ = 0, 512, 1024, 1536, 2048

    def rseg(s):
        return R[0:1, s:s + 512]

    def ln_stats(src_res, src_tile, sq_eng, with_t=False):
        for c in range(NCH):
            st = Stile[c % 2]
            rs = rS[c % 2]
            if sq_eng == "scalar":
                kb.op("scalar", lambda E, c=c, st=st: E.activation(
                    st[:], c16(src_tile, c), ACT.Square),
                      reads=(src_res[c],), writes=(rs,))
            else:
                kb.op(sq_eng, lambda E, c=c, st=st: E.tensor_mul(
                    st[:], c16(src_tile, c), c16(src_tile, c)),
                      reads=(src_res[c],), writes=(rs,))
            mm(pb[0][0:1, :], onec[:], c16(src_tile, c), 0,
               c == 0, c == NCH - 1, (src_res[c], rOnes))
            mm(pb[1][0:1, :], onec[:], st[:], 1, c == 0, c == NCH - 1,
               (rs, rOnes))
        kb.op("vector", lambda E: E.tensor_scalar_mul(
            rseg(C_), pb[0][0:1, :], 1.0 / D), reads=(rB[0],), writes=(rRow,))
        kb.op("vector", lambda E: E.tensor_scalar_mul(
            rseg(A_), pb[1][0:1, :], 1.0 / D), reads=(rB[1],), writes=(rRow,))
        kb.op("vector", lambda E: E.tensor_mul(
            rseg(B_), rseg(C_), rseg(C_)), reads=(rRow,), writes=(rRow,))
        kb.op("vector", lambda E: E.scalar_tensor_tensor(
            rseg(D_), rseg(A_), float(EPS), rseg(B_), ALU.add, ALU.subtract),
              reads=(rRow,), writes=(rRow,))
        kb.op("scalar", lambda E: E.activation(
            rseg(A_), rseg(D_), ACT.Sqrt), reads=(rRow,), writes=(rRow,))
        kb.op("vector", lambda E: E.reciprocal(
            rseg(B_), rseg(A_)), reads=(rRow,), writes=(rRow,))
        if not with_t:
            return
        kb.op("vector", lambda E: E.tensor_scalar(
            rseg(E_), rseg(D_), float(1.0 + EPS), float(-EPS),
            ALU.mult, ALU.add), reads=(rRow,), writes=(rRow,))
        kb.op("scalar", lambda E: E.activation(
            rseg(D_), rseg(E_), ACT.Sqrt), reads=(rRow,), writes=(rRow,))
        kb.op("vector", lambda E: E.reciprocal(
            rseg(A_), rseg(D_)), reads=(rRow,), writes=(rRow,))
        kb.op("vector", lambda E: E.tensor_mul(
            rseg(E_), rseg(C_), rseg(A_)), reads=(rRow,), writes=(rRow,))
        kb.op("vector", lambda E: E.tensor_mul(
            rseg(D_), rseg(D_), rseg(B_)), reads=(rRow,), writes=(rRow,))

    def broadcast(row_ap, row_res, dst_tile, dst_res, bank=0):
        mm(pb[bank][:], oner[:], row_ap, bank, True, True, (row_res, rOnes))
        kb.op("vector", lambda E: E.tensor_copy(dst_tile[:], pb[bank][:]),
              reads=(rB[bank],), writes=(dst_res,))

    for g in range(4):
        kb.mark(f"phA_g{g}")
        own = g < 2
        for t in range(4):
            slot = t % 2
            dma("sync", f"d_s{slot}", WP[:, slot * 2048:(slot + 1) * 2048],
                xin[(4 * g + t) * 128:(4 * g + t + 1) * 128, :],
                (), (rWP[slot],))
            for cb in range(4):
                bank = 4 + (cb % 2)
                for c4 in range(4):
                    c = cb * 4 + c4
                    kb.op("tensor", lambda E, slot=slot, c=c, bank=bank, c4=c4:
                          E.transpose(pb[bank][:, c4 * 128:(c4 + 1) * 128],
                                      WP[:, slot * 2048 + c * 128:
                                         slot * 2048 + (c + 1) * 128],
                                      identt[:]),
                          reads=(rWP[slot], rConst), writes=(rB[bank],))
                def ev(E, cb=cb, t=t, bank=bank):
                    dst = bass.AP(T1, cb * 4 * 512 + t * 128,
                                  [[8192, 128], [512, 4], [1, 128]])
                    return E.tensor_copy(dst, pb[bank][:])
                kb.op("vector", ev, reads=(rB[bank],),
                      writes=tuple(rT1[cb * 4 + i] for i in range(4)))
        ln_stats(rT1, T1, "gpsimd", with_t=True)
        if own:
            dma("gpsimd", "d_aca", acr[2 * g:2 * g + 1, :], rseg(A_),
                (rRow,), (rAcr,))
            dma("gpsimd", "d_acb", acr[2 * g + 1:2 * g + 2, :], rseg(E_),
                (rRow,), (rAcr,))
        broadcast(rseg(A_), rRow, bb0, rBB[0], bank=0)
        broadcast(rseg(E_), rRow, bb1, rBB[1], bank=1)
        if own:
            broadcast(rseg(D_), rRow, bb2[:, 0:512], rBB[2], bank=0)
        for c in range(NCH):
            sA, rA_ = Stile[c % 2], rS[c % 2]
            sB, rB_ = Stile[2 + c % 2], rS[2 + c % 2]
            kb.op("vector", lambda E, c=c, sA=sA: E.tensor_mul(
                sA[:], c16(T1, c), bb0[:]), reads=(rT1[c], rBB[0]), writes=(rA_,))
            kb.op("vector", lambda E, sA=sA, sB=sB: E.tensor_sub(
                sB[:], sA[:], bb1[:]), reads=(rA_, rBB[1]), writes=(rB_,))
            kb.op("scalar", lambda E, c=c, sB=sB: E.activation(
                c16(T2, c), sB[:], ACT.Identity, bias=b1t[:, c:c + 1],
                scale=g1t[:, c:c + 1]), reads=(rB_, rConst), writes=(rT2[c],))
            if own:
                kb.op("vector", lambda E, sA=sA, sB=sB: E.tensor_mul(
                    sA[:], sB[:], bb2[:, 0:512]), reads=(rB_, rBB[2]),
                      writes=(rA_,))
                dma("gpsimd", f"d_tf{g}{c % 2}",
                    tfm[c * 128:(c + 1) * 128, g * 512:(g + 1) * 512], sA[:],
                    (rA_,), (rTfm[g],))
                dma("gpsimd", f"d_xf{g}{c % 2}",
                    xfm[c * 128:(c + 1) * 128, g * 512:(g + 1) * 512],
                    c16(T1, c), (rT1[c],), (rXfm[g],))
        for mkv in range(2):
            slot = 2 + mkv
            dma("sync", f"d_s{slot}", WP[:, slot * 2048:(slot + 1) * 2048],
                colblock(wkv, mkv * 128), (), (rWP[slot],))
            prewait("tensor", rT2 + [rWP[slot]])
            for c in range(NCH):
                mm(pb[2 + mkv][:],
                   WP[:, slot * 2048 + c * 128:slot * 2048 + (c + 1) * 128],
                   c16(T2, c), 2 + mkv, c == 0, c == NCH - 1,
                   (rWP[slot], rT2[c]))
            if mkv == 0:
                kb.op("vector", lambda E, g=g: E.tensor_scalar(
                    kbuf[:, g * 512:(g + 1) * 512], pb[2][:], bkvt[:, 0:1],
                    SCALE, ALU.add, ALU.mult),
                      reads=(rB[2], rConst), writes=(rK[g],))
            else:
                kb.op("scalar", lambda E: E.activation(
                    S2[:], pb[3][:], ACT.Identity, bias=bkvt[:, 1:2]),
                      reads=(rB[3], rConst), writes=(rS[2],))
                for tt in range(4):
                    kb.op("tensor", lambda E, tt=tt: E.transpose(
                        pb[7][:, tt * 128:(tt + 1) * 128],
                        S2[:, tt * 128:(tt + 1) * 128], identt[:]),
                          reads=(rS[2], rConst), writes=(rB[7],))
                kb.op("vector", lambda E, g=g: E.tensor_copy(
                    vt[:, g * 512:(g + 1) * 512], pb[7][:]),
                      reads=(rB[7],), writes=(rV[g],))
        if own:
            flush([f"d_tf{g}0", f"d_tf{g}1"], (rTfm[g],),
                  clean_res=rS)
            flush([f"d_xf{g}0", f"d_xf{g}1"], (rXfm[g],),
                  clean_res=rT1)
    flush(["d_aca", "d_acb"], (rAcr,), clean_res=(rRow,))
    dma("sync", "d_init", maskst[:], masksi, (rBB[2],), (rConst, rBB[2]))

    def quant_pass1():
        kb.mark("quant1")
        for rt in range(NCH):
            slot = 2 + rt % 2
            dma("sync", f"d_s{slot}", WP[:, slot * 2048:(slot + 1) * 2048],
                wfn[rt * 128:(rt + 1) * 128, :], (), (rWP[slot],))
            kb.op("vector", lambda E, rt=rt, slot=slot: E.reduce_sum(
                absc[:, rt:rt + 1], WP[:, slot * 2048:(slot + 1) * 2048],
                axis=AX.X, apply_absolute_value=True),
                  reads=(rWP[slot],), writes=(rBB[0],))
        kb.op("vector", lambda E: E.reduce_sum(
            absc[:, NCH:NCH + 1], absc[:, 0:NCH], axis=AX.X),
              reads=(rBB[0],), writes=(rBB[0],))
        mm(pb[0][0:1, 0:1], onec[:], absc[:, NCH:NCH + 1], 0, True, True,
           (rBB[0], rOnes))
        kb.op("vector", lambda E: E.tensor_scalar(
            R[0:1, 0:1], pb[0][0:1, 0:1], float(1.0 / (D * D)), 1e-10,
            ALU.mult, ALU.add), reads=(rB[0],), writes=(rRow,))
        kb.op("vector", lambda E: E.reciprocal(R[0:1, 1:2], R[0:1, 0:1]),
              reads=(rRow,), writes=(rRow,))
        kb.op("vector", lambda E: E.tensor_scalar_add(
            R[0:1, 0:1], R[0:1, 0:1], -1e-10), reads=(rRow,), writes=(rRow,))
        mm(pb[0][:, 0:2], oner[:], R[0:1, 0:2], 0, True, True, (rRow, rOnes))
        kb.op("vector", lambda E: E.tensor_copy(sicol[:], pb[0][:, 0:2]),
              reads=(rB[0],), writes=(rSic,))

    def quant_pass2():
        kb.mark("quant2")
        for rt in range(NCH):
            yield
            slot = 2 + rt % 2
            wsl = WP[:, slot * 2048:(slot + 1) * 2048]
            dma("sync", f"d_s{slot}", wsl, wfn[rt * 128:(rt + 1) * 128, :],
                (), (rWP[slot],))
            for sp in range(4):
                wst = wsl[:, sp * 512:(sp + 1) * 512]
                sA, rA_ = Stile[sp % 2], rS[sp % 2]
                sB, rB_ = Stile[2 + sp % 2], rS[2 + sp % 2]
                kb.op("vector", lambda E, wst=wst, sA=sA: E.tensor_scalar(
                    sA[:], wst, sicol[:, 1:2], 0.5, ALU.mult, ALU.is_gt),
                      reads=(rWP[slot], rSic), writes=(rA_,))
                kb.op("vector", lambda E, wst=wst, sB=sB: E.tensor_scalar(
                    sB[:], wst, sicol[:, 1:2], -0.5, ALU.mult, ALU.is_lt),
                      reads=(rWP[slot], rSic), writes=(rB_,))
                kb.op("vector", lambda E, wst=wst, sA=sA, sB=sB: E.tensor_sub(
                    wst, sA[:], sB[:]), reads=(rA_, rB_), writes=(rWP[slot],))
            for cb in range(4):
                bank = 4 + (cb % 2)
                for c4 in range(4):
                    c = cb * 4 + c4
                    kb.op("tensor", lambda E, wsl=wsl, c=c, c4=c4, bank=bank:
                          E.transpose(pb[bank][:, c4 * 128:(c4 + 1) * 128],
                                      wsl[:, c * 128:(c + 1) * 128], identt[:]),
                          reads=(rWP[slot], rConst), writes=(rB[bank],))
                st4 = Stile[2 + cb % 2]
                rs4 = rS[2 + cb % 2]
                kb.op("vector", lambda E, bank=bank, st4=st4: E.tensor_copy(
                    st4[:], pb[bank][:]), reads=(rB[bank],), writes=(rs4,))
                def wqt_out(cb=cb, rt=rt):
                    return bass.AP(wqt.tensor, (cb * 4 * 128) * D + rt * 128,
                                   [[D, 128], [128 * D, 4], [1, 128]])
                dma("gpsimd", f"d_wqt{cb % 2}", wqt_out(),
                    st4[:].rearrange("p (c j) -> p c j", c=4),
                    (rs4,), (rWqt,))
        flush(["d_wqt0", "d_wqt1"], (rWqt,), clean_res=rS)
        yield

    def c16r(t, c):
        return RTW[id(t)][:, c * 512:(c + 1) * 512]

    def wslot(i):
        return WP[:, i * 2048:(i + 1) * 2048]

    def wslotr(i):
        return WPR[:, i * 2048:(i + 1) * 2048]

    wctr = [0]

    def nslot():
        s = wctr[0] % 4
        wctr[0] += 1
        return s

    for g in range(2):
        W = WIDTH[g]
        vis = VIS[g]
        diag_ci = DIAG_CI[g]
        nt128 = W // 128

        kb.mark(f"hrec_g{g}")
        dma("gpsimd", "d_r0", rseg(A_), acr[2 * g:2 * g + 1, :],
            (rAcr, rRow), (rRow,))
        dma("gpsimd", "d_r1", rseg(B_), acr[2 * g + 1:2 * g + 2, :],
            (rAcr, rRow), (rRow,))
        broadcast(rseg(A_), rRow, bb0, rBB[0], bank=0)
        broadcast(rseg(B_), rRow, bb1, rBB[1], bank=1)
        for c in range(NCH):
            sA, rA_ = Stile[c % 2], rS[c % 2]
            sB, rB_ = Stile[2 + c % 2], rS[2 + c % 2]
            dma("gpsimd", f"d_r{c % 2}", sA[:],
                xfm[c * 128:(c + 1) * 128, g * 512:(g + 1) * 512],
                (rXfm[g],), (rA_,))
            kb.op("vector", lambda E, sA=sA, sB=sB: E.tensor_mul(
                sB[:], sA[:], bb0[:]), reads=(rA_, rBB[0]), writes=(rB_,))
            kb.op("vector", lambda E, sB=sB: E.tensor_sub(
                sB[:], sB[:], bb1[:]), reads=(rB_, rBB[1]), writes=(rB_,))
            kb.op("scalar", lambda E, c=c, sB=sB: E.activation(
                c16(T1, c), sB[:], ACT.Identity, bias=b1t[:, c:c + 1],
                scale=g1t[:, c:c + 1]), reads=(rB_, rConst), writes=(rT1[c],))

        if g == 0:
            quant_pass1()

        kb.mark(f"q_g{g}")
        for m in range(NH):
            slot = nslot()
            bank = 2 + m % 2
            dma("sync", f"d_s{slot}", WP[:, slot * 2048:(slot + 1) * 2048],
                colblock(wq, m * 128), (), (rWP[slot],))
            prewait("tensor", rT1 + [rWP[slot]])
            for c in range(NCH):
                mm(pb[bank][:],
                   WP[:, slot * 2048 + c * 128:slot * 2048 + (c + 1) * 128],
                   c16(T1, c), bank, c == 0, c == NCH - 1, (rWP[slot], rT1[c]))
            kb.op("scalar", lambda E, m=m, bank=bank: E.activation(
                c16(T2, m), pb[bank][:], ACT.Identity, bias=bqt[:, m:m + 1]),
                  reads=(rB[bank], rConst), writes=(rT2[m],))

        q2gen = quant_pass2() if g == 0 else None
        kb.mark(f"attn_g{g}")

        def emit_scores(h):
            for st in range(4):
                srow0 = (st % 2) * 2048
                sres = [rT4[(st % 2) * 4 + i] for i in range(4)]
                for ci, ch in enumerate(vis):
                    bank = 4 + ci % 2
                    mm(pb[bank][:],
                       T2[:, h * 512 + st * 128:h * 512 + (st + 1) * 128],
                       kbuf[:, ch * 512:(ch + 1) * 512], bank, True, True,
                       (rT2[h], rK[ch]))
                    if ci == diag_ci:
                        kb.op("vector", lambda E, bank=bank, srow0=srow0,
                              ci=ci, st=st: E.tensor_add(
                                  T4[:, srow0 + ci * 512:srow0 + (ci + 1) * 512],
                                  pb[bank][:],
                                  maskst[:, 384 - 128 * st:896 - 128 * st]),
                              reads=(rB[bank], rConst), writes=(sres[ci],))
                    else:
                        kb.op("scalar", lambda E, bank=bank, srow0=srow0,
                              ci=ci, g=g, ch=ch: E.activation(
                                  T4[:, srow0 + ci * 512:srow0 + (ci + 1) * 512],
                                  pb[bank][:], ACT.Identity,
                                  bias=cmaskt[:, g * 4 + ch:g * 4 + ch + 1]),
                              reads=(rB[bank], rConst), writes=(sres[ci],))
                par2 = st % 2
                prow0 = 4096 + par2 * 2048
                kb.op("vector", lambda E, srow0=srow0, W=W, par2=par2:
                      E.reduce_max(negm[:, par2:par2 + 1],
                                   T4[:, srow0:srow0 + W], axis=AX.X,
                                   negate=True),
                      reads=tuple(sres), writes=(rNegm[par2],))
                pres = [rT4[8 + par2 * 4 + i] for i in range(4)]
                kb.op("scalar", lambda E, srow0=srow0, W=W, par2=par2,
                      prow0=prow0: E.activation(
                          T4[:, prow0:prow0 + W], T4[:, srow0:srow0 + W],
                          ACT.Exp, bias=negm[:, par2:par2 + 1],
                          accum_out=lsum[:, par2:par2 + 1]),
                      reads=tuple(sres) + (rNegm[par2],),
                      writes=tuple(pres) + (rLsum[par2],))
                kb.op("vector", lambda E, par2=par2: E.reciprocal(
                    lrec[:, par2:par2 + 1], lsum[:, par2:par2 + 1]),
                      reads=(rLsum[par2],), writes=(rLrec[par2],))
                kb.op("vector", lambda E, st=st, W=W, par2=par2, prow0=prow0:
                      E.tensor_scalar_mul(
                          T3[:, st * 2048:st * 2048 + W],
                          T4[:, prow0:prow0 + W], lrec[:, par2:par2 + 1]),
                      reads=tuple(pres) + (rLrec[par2],),
                      writes=tuple(rT3[st * 4 + i] for i in range(4)))

        def emit_pv(h):
            pend = None
            for jj in range(nt128):
                true128 = vis[jj // 4] * 4 + (jj % 4)
                par = jj % 2
                ptbank = 6 if par == 0 else 0
                for st in range(4):
                    kb.op("tensor", lambda E, st=st, jj=jj, ptbank=ptbank:
                          E.transpose(
                              pb[ptbank][:, st * 128:(st + 1) * 128],
                              T3[:, st * 2048 + jj * 128:st * 2048 + (jj + 1) * 128],
                              identt[:]),
                          reads=(rT3[st * 4 + jj // 4], rConst),
                          writes=(rB[ptbank],))
                ptt = bb0 if par == 0 else bb1
                ev_eng = "scalar" if par == 0 else "vector"
                if ev_eng == "scalar":
                    kb.op("scalar", lambda E, ptt=ptt, ptbank=ptbank:
                          E.activation(ptt[:], pb[ptbank][:], ACT.Identity),
                          reads=(rB[ptbank],), writes=(rBB[par],))
                else:
                    kb.op("vector", lambda E, ptt=ptt, ptbank=ptbank:
                          E.tensor_copy(ptt[:], pb[ptbank][:]),
                          reads=(rB[ptbank],), writes=(rBB[par],))
                if pend is not None:
                    mm(pb[7][:], pend[0], pend[1], 7, pend[2], pend[3], pend[4])
                pend = (vt[:, true128 * 128:(true128 + 1) * 128], ptt[:],
                        jj == 0, jj == nt128 - 1, (rV[true128 // 4], rBB[par]))
            mm(pb[7][:], pend[0], pend[1], 7, pend[2], pend[3], pend[4])
            kb.op("scalar", lambda E, h=h: E.activation(
                c16(T1, h), pb[7][:], ACT.Identity),
                  reads=(rB[7],), writes=(rT1[h],))

        for h in range(NH):
            if q2gen is not None:
                next(q2gen, None)
            emit_scores(h)
            emit_pv(h)
        if q2gen is not None:
            for _ in q2gen:
                pass

        for c in range(NCH):
            dma("gpsimd", f"d_r{c % 2}", c16(T3, c),
                tfm[c * 128:(c + 1) * 128, g * 512:(g + 1) * 512],
                (rTfm[g],), (rT3[c],))

        kb.mark(f"wo_g{g}")
        for m in range(NH):
            slot = nslot()
            bank = 2 + m % 2
            dma("sync", f"d_s{slot}", WP[:, slot * 2048:(slot + 1) * 2048],
                colblock(wo, m * 128), (), (rWP[slot],))
            prewait("tensor", rT1 + [rWP[slot]])
            for c in range(NCH):
                mm(pb[bank][:],
                   WP[:, slot * 2048 + c * 128:slot * 2048 + (c + 1) * 128],
                   c16(T1, c), bank, c == 0, c == NCH - 1, (rWP[slot], rT1[c]))
            sA, rA_ = Stile[m % 2], rS[m % 2]
            kb.op("scalar", lambda E, m=m, bank=bank, sA=sA: E.activation(
                sA[:], pb[bank][:], ACT.Identity, bias=bot[:, m:m + 1]),
                  reads=(rB[bank], rConst), writes=(rA_,))
            kb.op("vector", lambda E, m=m, sA=sA: E.tensor_add(
                c16(T2, m), sA[:], c16(T3, m)), reads=(rA_, rT3[m]),
                  writes=(rT2[m],))

        kb.mark(f"ln2_g{g}")
        ln_stats(rT2, T2, "scalar")
        broadcast(rseg(C_), rRow, bb0, rBB[0], bank=0)
        broadcast(rseg(B_), rRow, bb1, rBB[1], bank=1)
        for c in range(NCH):
            sA, rA_ = Stile[c % 2], rS[c % 2]
            sB, rB_ = Stile[2 + c % 2], rS[2 + c % 2]
            kb.op("vector", lambda E, c=c, sA=sA: E.tensor_sub(
                sA[:], c16(T2, c), bb0[:]), reads=(rT2[c], rBB[0]), writes=(rA_,))
            kb.op("vector", lambda E, sA=sA, sB=sB: E.tensor_mul(
                sB[:], sA[:], bb1[:]), reads=(rA_, rBB[1]), writes=(rB_,))
            kb.op("scalar", lambda E, c=c, sB=sB: E.activation(
                c16(T1, c), sB[:], ACT.Identity, bias=b2t[:, c:c + 1],
                scale=g2t[:, c:c + 1]), reads=(rB_, rConst), writes=(rT1[c],))

        kb.mark(f"ffn_g{g}")
        for icg in range(4):
            for ici in range(16):
                ic = icg * 16 + ici
                slot = nslot()
                bank = 2 + ici % 2
                dma("sync", f"d_s{slot}", WP[:, slot * 2048:(slot + 1) * 2048],
                    colblock(wfc, ic * 128), (), (rWP[slot],))
                prewait("tensor", rT1 + [rWP[slot]])
                for c in range(NCH):
                    mm(pb[bank][:],
                       WP[:, slot * 2048 + c * 128:slot * 2048 + (c + 1) * 128],
                       c16(T1, c), bank, c == 0, c == NCH - 1,
                       (rWP[slot], rT1[c]))
                kb.op("scalar", lambda E, ic=ic, ici=ici, bank=bank: E.activation(
                    c16(T3, ici), pb[bank][:], ACT.Gelu_apprx_tanh,
                    bias=bfct[:, ic:ic + 1]), reads=(rB[bank], rConst),
                      writes=(rT3[ici],))
            for m in range(NH):
                slot = nslot()
                bank = 4 + m % 2
                wprap = bass.AP(wpr.tensor, (icg * 2048) * D + m * 128,
                                [[D, 128], [128 * D, 16], [1, 128]])
                dma("sync", f"d_s{slot}", WP[:, slot * 2048:(slot + 1) * 2048],
                    wprap, (), (rWP[slot],))
                prewait("tensor", rT3 + [rWP[slot]])
                for k in range(16):
                    mm(pb[bank][:],
                       WP[:, slot * 2048 + k * 128:slot * 2048 + (k + 1) * 128],
                       c16(T3, k), bank, k == 0, k == 15, (rWP[slot], rT3[k]))
                if icg == 0:
                    kb.op("vector", lambda E, m=m, bank=bank: E.tensor_copy(
                        c16(T4, m), pb[bank][:]), reads=(rB[bank],),
                          writes=(rT4[m],))
                else:
                    kb.op("vector", lambda E, m=m, bank=bank: E.tensor_add(
                        c16(T4, m), c16(T4, m), pb[bank][:]),
                          reads=(rB[bank], rT4[m]), writes=(rT4[m],))

        kb.mark(f"x1_g{g}")
        for c in range(NCH):
            sA, rA_ = Stile[c % 2], rS[c % 2]
            sB, rB_ = Stile[2 + c % 2], rS[2 + c % 2]
            dma("gpsimd", f"d_r{c % 2}", sB[:],
                xfm[c * 128:(c + 1) * 128, g * 512:(g + 1) * 512],
                (rXfm[g],), (rB_,))
            kb.op("vector", lambda E, c=c, sA=sA: E.scalar_tensor_tensor(
                sA[:], c16(T4, c), bprt[:, c:c + 1], c16(T2, c),
                ALU.add, ALU.add), reads=(rT4[c], rT2[c], rConst), writes=(rA_,))
            kb.op("vector", lambda E, c=c, sA=sA, sB=sB: E.tensor_add(
                c16(T1, c), sA[:], sB[:]), reads=(rA_, rB_), writes=(rT1[c],))

        kb.mark(f"ln3_g{g}")
        ln_stats(rT1, T1, "scalar")
        broadcast(rseg(C_), rRow, bb0, rBB[0], bank=0)
        broadcast(rseg(B_), rRow, bb1, rBB[1], bank=1)
        for c in range(NCH):
            sA, rA_ = Stile[c % 2], rS[c % 2]
            kb.op("vector", lambda E, c=c, sA=sA: E.tensor_sub(
                sA[:], c16(T1, c), bb0[:]), reads=(rT1[c], rBB[0]), writes=(rA_,))
            kb.op("vector", lambda E, c=c, sA=sA: E.tensor_mul(
                c16(T2, c), sA[:], bb1[:]), reads=(rA_, rBB[1]), writes=(rT2[c],))

        kb.mark(f"fin_g{g}")
        for m in range(NH):
            slot = nslot()
            bank = 2 + m % 2
            dma("sync", f"d_s{slot}", WP[:, slot * 2048:(slot + 1) * 2048],
                colblock(wqt, m * 128), (rWqt,), (rWP[slot],))
            prewait("tensor", rT2 + [rWP[slot]])
            for k in range(16):
                mm(pb[bank][:],
                   WP[:, slot * 2048 + k * 128:slot * 2048 + (k + 1) * 128],
                   c16(T2, k), bank, k == 0, k == 15, (rWP[slot], rT2[k]))
            sA, rA_ = Stile[m % 2], rS[m % 2]
            kb.op("vector", lambda E, bank=bank, sA=sA, m=m: E.tensor_scalar(
                sA[:], pb[bank][:], sicol[:, 0:1], bfnt[:, m:m + 1],
                ALU.mult, ALU.add), reads=(rB[bank], rConst, rSic),
                  writes=(rA_,))
            kb.op("vector", lambda E, m=m, sA=sA: E.tensor_add(
                c16(T3, m), sA[:], c16(T1, m)), reads=(rA_, rT1[m]),
                  writes=(rT3[m],))

        kb.mark(f"out_g{g}")
        for t in range(4):
            par = t % 2
            ost = T4[:, par * 4096:par * 4096 + 2048]
            ores = [rT4[par * 8 + i] for i in range(4)]
            for cb in range(4):
                bank = 4 + cb % 2
                for c4 in range(4):
                    c = cb * 4 + c4
                    kb.op("tensor", lambda E, c=c, t=t, c4=c4, bank=bank:
                          E.transpose(pb[bank][:, c4 * 128:(c4 + 1) * 128],
                                      T3[:, c * 512 + t * 128:
                                         c * 512 + (t + 1) * 128],
                                      identt[:]),
                          reads=(rT3[c], rConst), writes=(rB[bank],))
                kb.op("vector", lambda E, cb=cb, bank=bank, ost=ost:
                      E.tensor_copy(ost[:, cb * 512:(cb + 1) * 512],
                                    pb[bank][:]),
                      reads=(rB[bank],), writes=(ores[cb],))
            dma("gpsimd", f"d_o{par}",
                out[(g * 512 + t * 128):(g * 512 + (t + 1) * 128), :],
                ost, tuple(ores), ())

    for par in range(2):
        tot = kb.cnt.get(f"d_o{par}", 0)
        if tot:
            kb.barrier_wait("gpsimd", f"d_o{par}", tot)

    sems_needed = set(kb.cnt.keys()) | set(kb.engsem.values())
    ctx = kb.realize(sems_needed)
    ctx.close()
    outer.close()
    build_program.marks = kb.marks
    build_program.streams = {e: [i for i in kb.stream[e]] for e in ENGS}
    return nc


_PROG = None
ORDER = {0: [0, 3, 1, 2], 1: [1, 2, 0, 3]}


def _host_inputs(inputs):
    x = np.asarray(inputs["L0_attn_hidden_states"], np.float32)

    def colmaj(v, k):
        return np.ascontiguousarray(np.asarray(v, np.float32).reshape(k, 128).T)

    shared = {
        "wq": np.ascontiguousarray(np.asarray(inputs["L0_attn_wq"], np.float32)),
        "wkv": np.ascontiguousarray(np.asarray(inputs["L0_attn_wkv"], np.float32)),
        "wo": np.ascontiguousarray(np.asarray(inputs["L0_attn_wo"], np.float32)),
        "wfc": np.ascontiguousarray(np.asarray(inputs["L0_attn_wfc"], np.float32)),
        "wpr": np.ascontiguousarray(np.asarray(inputs["L0_attn_wproj"], np.float32)),
        "wfn": np.ascontiguousarray(np.asarray(inputs["L0_ffn_weight"], np.float32)),
        "g1": colmaj(inputs["L0_attn_ln1_g"], NCH),
        "b1": colmaj(inputs["L0_attn_ln1_b"], NCH),
        "g2": colmaj(inputs["L0_attn_ln2_g"], NCH),
        "b2": colmaj(inputs["L0_attn_ln2_b"], NCH),
        "bq": colmaj(inputs["L0_attn_bq"], NCH),
        "bkv": colmaj(inputs["L0_attn_bkv"], 2),
        "bo": colmaj(inputs["L0_attn_bo"], NCH),
        "bfc": colmaj(inputs["L0_attn_bfc"], NIC),
        "bpr": colmaj(inputs["L0_attn_bproj"], NCH),
        "bfn": colmaj(inputs["L0_ffn_bias"], NCH),
        "ident": np.eye(128, dtype=np.float32),
    }
    s_idx = np.arange(128)[:, None]
    u_idx = np.arange(896)[None, :]
    shared["masks"] = np.ascontiguousarray(
        np.where(u_idx - 384 <= s_idx, 0.0, NEG).astype(np.float32))

    in_maps = []
    for core in range(8):
        b, half = core // 2, core % 2
        order = ORDER[half]
        xb = np.ascontiguousarray(
            np.concatenate([x[b, 512 * q:512 * (q + 1)] for q in order], axis=0))
        cm = np.zeros(8, np.float32)
        for g in range(2):
            qg = order[g]
            for ci, ch in enumerate(VIS[g]):
                if ci == DIAG_CI[g]:
                    cm[g * 4 + ch] = 0.0
                else:
                    cm[g * 4 + ch] = 0.0 if order[ch] < qg else NEG
        cmask = np.ascontiguousarray(
            np.broadcast_to(cm[None, :], (128, 8))).astype(np.float32)
        m = {"x": xb, "cmask": cmask}
        m.update(shared)
        in_maps.append(m)
    return in_maps


def kernel(**inputs):
    global _PROG
    in_maps = _host_inputs(inputs)
    if _PROG is None:
        _PROG = build_program()
    res = run_bass_kernel_spmd(_PROG, in_maps, core_ids=list(range(8)))
    outx = np.empty((4, 2048, D), np.float32)
    for core in range(8):
        b, half = core // 2, core % 2
        order = ORDER[half]
        o = np.asarray(res.results[core]["out"])
        outx[b, 512 * order[0]:512 * (order[0] + 1)] = o[0:512]
        outx[b, 512 * order[1]:512 * (order[1] + 1)] = o[512:1024]
    return outx


# version 13
# speedup vs baseline: 1.1247x; 1.1247x over previous
import numpy as np
import concourse.bass as bass
import concourse.mybir as mybir
from concourse.bass_utils import run_bass_kernel_spmd

F32 = mybir.dt.float32
F32R = mybir.dt.float32r
AX = mybir.AxisListType
ALU = mybir.AluOpType
ACT = mybir.ActivationFunctionType

D = 2048
NCH = 16
NH = 16
INTER = 8192
NIC = 64
SCALE = float(128) ** -0.5
EPS = 1e-5
NEG = -3.0e38

ENGS = ("sync", "tensor", "vector", "scalar", "gpsimd")
DEBUG_DUMP = False

VIS = [[0, 2], [0, 1, 2, 3]]
DIAG_CI = [0, 1]
WIDTH = [1024, 2048]


class Res:
    __slots__ = ("w", "r", "psum")

    def __init__(self, psum=False):
        self.w = None
        self.r = {}
        self.psum = psum


class KB:
    def __init__(self, nc):
        self.nc = nc
        self.stream = {e: [] for e in ENGS}
        self.cnt = {}
        self.waited = {e: {} for e in ENGS}
        self.engsem = {e: "e_" + e for e in ENGS}
        self.marks = []

    def _bump(self, sem, n):
        self.cnt[sem] = self.cnt.get(sem, 0) + n
        return self.cnt[sem]

    def _wait(self, eng, tok):
        sem, val = tok[0], tok[1]
        if self.waited[eng].get(sem, 0) >= val:
            return
        self.waited[eng][sem] = val
        self.stream[eng].append(("w", sem, val))

    def op(self, eng, fn, reads=(), writes=(), dma=None):
        op_psum = any(r.psum for r in tuple(reads) + tuple(writes))
        want = {}
        for r in reads:
            if r.w is not None:
                tok = r.w
                if eng == "tensor" and tok[2] == "tensor":
                    continue
                if want.get(tok[0], -1) < tok[1]:
                    want[tok[0]] = tok[1]
        for w in writes:
            toks = ([w.w] if w.w is not None else []) + list(w.r.values())
            for tok in toks:
                if dma is not None and tok[0] == dma and tok[2] == eng:
                    continue
                if eng == "tensor" and tok[2] == "tensor":
                    continue
                if want.get(tok[0], -1) < tok[1]:
                    want[tok[0]] = tok[1]
        for sem, val in want.items():
            self._wait(eng, (sem, val, None, False))
        if dma is not None:
            prev = self.cnt.get(dma, 0)
            if prev:
                self._wait(eng, (dma, prev, eng, False))
            val = self._bump(dma, 16)
            tok = (dma, val, eng, op_psum)
            self.stream[eng].append(("o", fn, dma, 16))
        else:
            sem = self.engsem[eng]
            val = self._bump(sem, 1)
            tok = (sem, val, eng, op_psum)
            self.stream[eng].append(("o", fn, sem, 1))
        for r in reads:
            old = r.r.get(tok[0])
            if old is None or old[1] < tok[1]:
                r.r[tok[0]] = tok
        for w in writes:
            w.w = tok
            w.r = {}
        return tok

    def mark(self, name):
        idx = len(self.marks)
        self.marks.append((name, {e: len(self.stream[e]) for e in ENGS},
                           dict(self.cnt)))
        self.stream["tensor"].append(("r", lambda E, i=idx: E.notification(i)))

    def barrier_wait(self, eng, sem, val):
        self._wait(eng, (sem, val, "x", False))

    def realize(self, sems_needed):
        nc = self.nc
        from contextlib import ExitStack
        ctx = ExitStack()
        sems = {n: ctx.enter_context(nc.semaphore(n)) for n in sorted(sems_needed)}
        block = ctx.enter_context(nc.Block())

        def runner(engname):
            def _run(e):
                for item in self.stream[engname]:
                    if item[0] == "w":
                        e.wait_ge(sems[item[1]], item[2])
                    elif item[0] == "r":
                        item[1](e)
                    else:
                        item[1](e).then_inc(sems[item[2]], item[3])
            return _run

        for engname in ENGS:
            getattr(block, engname)(runner(engname))
        return ctx


def build_program():
    nc = bass.Bass("TRN2", target_bir_lowering=False)

    xin = nc.dram_tensor("x", [2048, D], F32, kind="ExternalInput").ap()
    wq = nc.dram_tensor("wq", [D, D], F32, kind="ExternalInput").ap()
    wkv = nc.dram_tensor("wkv", [D, 256], F32, kind="ExternalInput").ap()
    wo = nc.dram_tensor("wo", [D, D], F32, kind="ExternalInput").ap()
    wfc = nc.dram_tensor("wfc", [D, INTER], F32, kind="ExternalInput").ap()
    wpr = nc.dram_tensor("wpr", [INTER, D], F32, kind="ExternalInput").ap()
    wfn = nc.dram_tensor("wfn", [D, D], F32, kind="ExternalInput").ap()
    g1i = nc.dram_tensor("g1", [128, NCH], F32, kind="ExternalInput").ap()
    b1i = nc.dram_tensor("b1", [128, NCH], F32, kind="ExternalInput").ap()
    g2i = nc.dram_tensor("g2", [128, NCH], F32, kind="ExternalInput").ap()
    b2i = nc.dram_tensor("b2", [128, NCH], F32, kind="ExternalInput").ap()
    bqi = nc.dram_tensor("bq", [128, NCH], F32, kind="ExternalInput").ap()
    bkvi = nc.dram_tensor("bkv", [128, 2], F32, kind="ExternalInput").ap()
    boi = nc.dram_tensor("bo", [128, NCH], F32, kind="ExternalInput").ap()
    bfci = nc.dram_tensor("bfc", [128, NIC], F32, kind="ExternalInput").ap()
    bpri = nc.dram_tensor("bpr", [128, NCH], F32, kind="ExternalInput").ap()
    bfni = nc.dram_tensor("bfn", [128, NCH], F32, kind="ExternalInput").ap()
    identi = nc.dram_tensor("ident", [128, 128], F32, kind="ExternalInput").ap()
    masksi = nc.dram_tensor("masks", [128, 896], F32, kind="ExternalInput").ap()
    cmaski = nc.dram_tensor("cmask", [128, 8], F32, kind="ExternalInput").ap()
    out = nc.dram_tensor("out", [1024, D], F32, kind="ExternalOutput").ap()
    xfm = nc.dram_tensor("xfm", [D, 1024], F32).ap()
    tfm = nc.dram_tensor("tfm", [D, 1024], F32).ap()
    wqt = nc.dram_tensor("wqt", [D, D], F32).ap()
    acr = nc.dram_tensor("acr", [4, 512], F32).ap()
    dbg = nc.dram_tensor("dbg", [384, 8192], F32).ap() if DEBUG_DUMP else None
    zfm = nc.dram_tensor("zfm", [D, 1024], F32).ap()
    afm = nc.dram_tensor("afm", [D, 1024], F32).ap()

    from contextlib import ExitStack
    outer = ExitStack()
    e = outer.enter_context

    T1 = e(nc.sbuf_tensor("T1", [128, 8192], F32))
    T2 = e(nc.sbuf_tensor("T2", [128, 8192], F32))
    T3 = e(nc.sbuf_tensor("T3", [128, 8192], F32))
    T4 = e(nc.sbuf_tensor("T4", [128, 8192], F32))
    WP = e(nc.sbuf_tensor("WP", [128, 4 * 2048], F32))
    T1R = nc.virtual_sbuf_tensor("T1R", [128, 8192], F32R)
    T2R = nc.virtual_sbuf_tensor("T2R", [128, 8192], F32R)
    T3R = nc.virtual_sbuf_tensor("T3R", [128, 8192], F32R)
    T4R = nc.virtual_sbuf_tensor("T4R", [128, 8192], F32R)
    WPR = nc.virtual_sbuf_tensor("WPR", [128, 4 * 2048], F32R)
    nc.back_virtual_tensor_with(T1R, T1)
    nc.back_virtual_tensor_with(T2R, T2)
    nc.back_virtual_tensor_with(T3R, T3)
    nc.back_virtual_tensor_with(T4R, T4)
    nc.back_virtual_tensor_with(WPR, WP)
    RTW = {id(T1): T1R, id(T2): T2R, id(T3): T3R, id(T4): T4R, id(WP): WPR}
    kbuf = e(nc.sbuf_tensor("kbuf", [128, 2048], F32))
    vt = e(nc.sbuf_tensor("vt", [128, 2048], F32))
    S0 = e(nc.sbuf_tensor("S0", [128, 512], F32))
    S1 = e(nc.sbuf_tensor("S1", [128, 512], F32))
    S2 = e(nc.sbuf_tensor("S2", [128, 512], F32))
    S3 = e(nc.sbuf_tensor("S3", [128, 512], F32))
    bb0 = e(nc.sbuf_tensor("bb0", [128, 512], F32))
    bb1 = e(nc.sbuf_tensor("bb1", [128, 512], F32))
    R = e(nc.sbuf_tensor("R", [1, 2560], F32))
    onec = e(nc.sbuf_tensor("onec", [128, 1], F32))
    oner = e(nc.sbuf_tensor("oner", [1, 128], F32))
    negm = e(nc.sbuf_tensor("negm", [128, 2], F32))
    lsum = e(nc.sbuf_tensor("lsum", [128, 2], F32))
    lrec = e(nc.sbuf_tensor("lrec", [128, 2], F32))
    g1t = e(nc.sbuf_tensor("g1t", [128, NCH], F32))
    b1t = e(nc.sbuf_tensor("b1t", [128, NCH], F32))
    g2t = e(nc.sbuf_tensor("g2t", [128, NCH], F32))
    b2t = e(nc.sbuf_tensor("b2t", [128, NCH], F32))
    bqt = e(nc.sbuf_tensor("bqt", [128, NCH], F32))
    bkvt = e(nc.sbuf_tensor("bkvt", [128, 2], F32))
    bot = e(nc.sbuf_tensor("bot", [128, NCH], F32))
    bfct = e(nc.sbuf_tensor("bfct", [128, NIC], F32))
    bprt = e(nc.sbuf_tensor("bprt", [128, NCH], F32))
    bfnt = e(nc.sbuf_tensor("bfnt", [128, NCH], F32))
    identt = e(nc.sbuf_tensor("identt", [128, 128], F32))
    maskst = e(nc.sbuf_tensor("maskst", [128, 896], F32))
    bb2 = maskst
    cmaskt = e(nc.sbuf_tensor("cmaskt", [128, 8], F32))
    sicol = e(nc.sbuf_tensor("sicol", [128, 2], F32))
    absc = e(nc.sbuf_tensor("absc", [128, NCH + 1], F32))

    pb = [e(nc.psum_tensor(f"pb{i}", [128, 512], F32)) for i in range(8)]

    kb = KB(nc)

    def slices(n):
        return [Res() for _ in range(n)]

    rT1, rT2, rT3, rT4 = slices(NCH), slices(NCH), slices(NCH), slices(NCH)
    rWP = slices(4)
    rK = slices(4)
    rV = slices(4)
    rPT = slices(2)
    rS = slices(4)
    rBB = slices(3)
    rRow = Res()
    rAcr = Res()
    rNegm = [Res(), Res()]
    rLsum = [Res(), Res()]
    rLrec = [Res(), Res()]
    rB = [Res(psum=True) for _ in range(8)]
    rConst = Res()
    rAbs = Res()
    rSic = Res()
    rXfm = [Res(), Res()]
    rTfm = [Res(), Res()]
    rWqt = Res()
    rOnes = Res()

    Stile = [S0, S1, S2, S3]

    def dma(eng, sem, outap, inap, reads, writes):
        return kb.op(eng, lambda E, o=outap, i=inap: E.dma_start(out=o, in_=i),
                     reads=reads, writes=writes, dma=sem)

    def flush(sems, dst_res, clean_res=()):
        for s in sems:
            kb.barrier_wait("gpsimd", s, kb.cnt.get(s, 0))
        tok = kb.op("gpsimd", lambda E: E.engine_nop(), writes=tuple(dst_res))
        for r in clean_res:
            r.r = {s: t for s, t in r.r.items() if not s.startswith("d_")}
            r.r[tok[0]] = tok
        return tok

    def mm(outap, lhs, rhs, bank, start, stop, reads):
        return kb.op("tensor",
                     lambda E, o=outap, l=lhs, r=rhs, st=start, sp=stop:
                     E.matmul(o, l, r, start=st, stop=sp),
                     reads=tuple(reads), writes=(rB[bank],))

    def prewait(eng, reslist):
        want = {}
        for r in reslist:
            if r.w is not None:
                s, v = r.w[0], r.w[1]
                if want.get(s, -1) < v:
                    want[s] = v
        for s, v in want.items():
            kb._wait(eng, (s, v, None, False))

    def colblock(w, col0):
        ld = int(w.shape[-1])
        return bass.AP(w.tensor, col0, [[ld, 128], [128 * ld, 16], [1, 128]])

    def c16(t, c, width=512):
        return t[:, c * width:(c + 1) * width]

    init_loads = [
        (g1t, g1i), (b1t, b1i), (g2t, g2i), (b2t, b2i), (bqt, bqi),
        (bkvt, bkvi), (bot, boi), (bfct, bfci), (bprt, bpri), (bfnt, bfni),
        (identt, identi), (cmaskt, cmaski),
    ]
    for dst, src_ in init_loads:
        dma("sync", "d_init", dst[:], src_, (), (rConst,))
    kb.op("gpsimd", lambda E: E.memset(onec[:], 1.0), writes=(rOnes,))
    kb.op("gpsimd", lambda E: E.memset(oner[:], 1.0), writes=(rOnes,))

    A_, B_, C_, D_, E_ = 0, 512, 1024, 1536, 2048

    def rseg(s):
        return R[0:1, s:s + 512]

    def ln_stats(src_res, src_tile, sq_eng, with_t=False):
        for c in range(NCH):
            st = Stile[c % 2]
            rs = rS[c % 2]
            if sq_eng == "scalar":
                kb.op("scalar", lambda E, c=c, st=st: E.activation(
                    st[:], c16(src_tile, c), ACT.Square),
                      reads=(src_res[c],), writes=(rs,))
            else:
                kb.op(sq_eng, lambda E, c=c, st=st: E.tensor_mul(
                    st[:], c16(src_tile, c), c16(src_tile, c)),
                      reads=(src_res[c],), writes=(rs,))
            mm(pb[0][0:1, :], onec[:], c16(src_tile, c), 0,
               c == 0, c == NCH - 1, (src_res[c], rOnes))
            mm(pb[1][0:1, :], onec[:], st[:], 1, c == 0, c == NCH - 1,
               (rs, rOnes))
        kb.op("vector", lambda E: E.tensor_scalar_mul(
            rseg(C_), pb[0][0:1, :], 1.0 / D), reads=(rB[0],), writes=(rRow,))
        kb.op("vector", lambda E: E.tensor_scalar_mul(
            rseg(A_), pb[1][0:1, :], 1.0 / D), reads=(rB[1],), writes=(rRow,))
        kb.op("vector", lambda E: E.tensor_mul(
            rseg(B_), rseg(C_), rseg(C_)), reads=(rRow,), writes=(rRow,))
        kb.op("vector", lambda E: E.scalar_tensor_tensor(
            rseg(D_), rseg(A_), float(EPS), rseg(B_), ALU.add, ALU.subtract),
              reads=(rRow,), writes=(rRow,))
        kb.op("scalar", lambda E: E.activation(
            rseg(A_), rseg(D_), ACT.Sqrt), reads=(rRow,), writes=(rRow,))
        kb.op("vector", lambda E: E.reciprocal(
            rseg(B_), rseg(A_)), reads=(rRow,), writes=(rRow,))
        if not with_t:
            return
        kb.op("vector", lambda E: E.tensor_scalar(
            rseg(E_), rseg(D_), float(1.0 + EPS), float(-EPS),
            ALU.mult, ALU.add), reads=(rRow,), writes=(rRow,))
        kb.op("scalar", lambda E: E.activation(
            rseg(D_), rseg(E_), ACT.Sqrt), reads=(rRow,), writes=(rRow,))
        kb.op("vector", lambda E: E.reciprocal(
            rseg(A_), rseg(D_)), reads=(rRow,), writes=(rRow,))
        kb.op("vector", lambda E: E.tensor_mul(
            rseg(E_), rseg(C_), rseg(A_)), reads=(rRow,), writes=(rRow,))
        kb.op("vector", lambda E: E.tensor_mul(
            rseg(D_), rseg(D_), rseg(B_)), reads=(rRow,), writes=(rRow,))

    def broadcast(row_ap, row_res, dst_tile, dst_res, bank=0):
        mm(pb[bank][:], oner[:], row_ap, bank, True, True, (row_res, rOnes))
        kb.op("vector", lambda E: E.tensor_copy(dst_tile[:], pb[bank][:]),
              reads=(rB[bank],), writes=(dst_res,))

    for g in range(4):
        kb.mark(f"phA_g{g}")
        own = g < 2
        for t in range(4):
            slot = t % 2
            dma("sync", f"d_s{slot}", WP[:, slot * 2048:(slot + 1) * 2048],
                xin[(4 * g + t) * 128:(4 * g + t + 1) * 128, :],
                (), (rWP[slot],))
            for cb in range(4):
                bank = 4 + (cb % 2)
                for c4 in range(4):
                    c = cb * 4 + c4
                    kb.op("tensor", lambda E, slot=slot, c=c, bank=bank, c4=c4:
                          E.transpose(pb[bank][:, c4 * 128:(c4 + 1) * 128],
                                      WP[:, slot * 2048 + c * 128:
                                         slot * 2048 + (c + 1) * 128],
                                      identt[:]),
                          reads=(rWP[slot], rConst), writes=(rB[bank],))
                def ev(E, cb=cb, t=t, bank=bank):
                    dst = bass.AP(T1, cb * 4 * 512 + t * 128,
                                  [[8192, 128], [512, 4], [1, 128]])
                    return E.tensor_copy(dst, pb[bank][:])
                kb.op("vector", ev, reads=(rB[bank],),
                      writes=tuple(rT1[cb * 4 + i] for i in range(4)))
        ln_stats(rT1, T1, "gpsimd", with_t=True)
        if own:
            dma("gpsimd", "d_aca", acr[2 * g:2 * g + 1, :], rseg(A_),
                (rRow,), (rAcr,))
            dma("gpsimd", "d_acb", acr[2 * g + 1:2 * g + 2, :], rseg(E_),
                (rRow,), (rAcr,))
        broadcast(rseg(A_), rRow, bb0, rBB[0], bank=0)
        broadcast(rseg(E_), rRow, bb1, rBB[1], bank=1)
        if own:
            broadcast(rseg(D_), rRow, bb2[:, 0:512], rBB[2], bank=0)
        for c in range(NCH):
            sA, rA_ = Stile[c % 2], rS[c % 2]
            sB, rB_ = Stile[2 + c % 2], rS[2 + c % 2]
            kb.op("vector", lambda E, c=c, sA=sA: E.tensor_mul(
                sA[:], c16(T1, c), bb0[:]), reads=(rT1[c], rBB[0]), writes=(rA_,))
            kb.op("vector", lambda E, sA=sA, sB=sB: E.tensor_sub(
                sB[:], sA[:], bb1[:]), reads=(rA_, rBB[1]), writes=(rB_,))
            kb.op("scalar", lambda E, c=c, sB=sB: E.activation(
                c16(T2, c), sB[:], ACT.Identity, bias=b1t[:, c:c + 1],
                scale=g1t[:, c:c + 1]), reads=(rB_, rConst), writes=(rT2[c],))
            if own:
                kb.op("vector", lambda E, sA=sA, sB=sB: E.tensor_mul(
                    sA[:], sB[:], bb2[:, 0:512]), reads=(rB_, rBB[2]),
                      writes=(rA_,))
                dma("gpsimd", f"d_tf{g}{c % 2}",
                    tfm[c * 128:(c + 1) * 128, g * 512:(g + 1) * 512], sA[:],
                    (rA_,), (rTfm[g],))
                dma("gpsimd", f"d_xf{g}{c % 2}",
                    xfm[c * 128:(c + 1) * 128, g * 512:(g + 1) * 512],
                    c16(T1, c), (rT1[c],), (rXfm[g],))
        for mkv in range(2):
            slot = 2 + mkv
            dma("sync", f"d_s{slot}", WP[:, slot * 2048:(slot + 1) * 2048],
                colblock(wkv, mkv * 128), (), (rWP[slot],))
            prewait("tensor", rT2 + [rWP[slot]])
            for c in range(NCH):
                mm(pb[2 + mkv][:],
                   WP[:, slot * 2048 + c * 128:slot * 2048 + (c + 1) * 128],
                   c16(T2, c), 2 + mkv, c == 0, c == NCH - 1,
                   (rWP[slot], rT2[c]))
            if mkv == 0:
                kb.op("vector", lambda E, g=g: E.tensor_scalar(
                    kbuf[:, g * 512:(g + 1) * 512], pb[2][:], bkvt[:, 0:1],
                    SCALE, ALU.add, ALU.mult),
                      reads=(rB[2], rConst), writes=(rK[g],))
            else:
                kb.op("scalar", lambda E: E.activation(
                    S2[:], pb[3][:], ACT.Identity, bias=bkvt[:, 1:2]),
                      reads=(rB[3], rConst), writes=(rS[2],))
                for tt in range(4):
                    kb.op("tensor", lambda E, tt=tt: E.transpose(
                        pb[7][:, tt * 128:(tt + 1) * 128],
                        S2[:, tt * 128:(tt + 1) * 128], identt[:]),
                          reads=(rS[2], rConst), writes=(rB[7],))
                kb.op("vector", lambda E, g=g: E.tensor_copy(
                    vt[:, g * 512:(g + 1) * 512], pb[7][:]),
                      reads=(rB[7],), writes=(rV[g],))
        if own:
            flush([f"d_tf{g}0", f"d_tf{g}1"], (rTfm[g],),
                  clean_res=rS)
            flush([f"d_xf{g}0", f"d_xf{g}1"], (rXfm[g],),
                  clean_res=rT1)
    flush(["d_aca", "d_acb"], (rAcr,), clean_res=(rRow,))
    dma("sync", "d_init", maskst[:], masksi, (rBB[2],), (rConst, rBB[2]))

    def quant_pass1():
        kb.mark("quant1")
        for rt in range(NCH):
            yield
            slot = 2 + rt % 2
            dma("sync", f"d_s{slot}", WP[:, slot * 2048:(slot + 1) * 2048],
                wfn[rt * 128:(rt + 1) * 128, :], (), (rWP[slot],))
            kb.op("vector", lambda E, rt=rt, slot=slot: E.reduce_sum(
                absc[:, rt:rt + 1], WP[:, slot * 2048:(slot + 1) * 2048],
                axis=AX.X, apply_absolute_value=True),
                  reads=(rWP[slot],), writes=(rAbs,))
        kb.op("vector", lambda E: E.reduce_sum(
            absc[:, NCH:NCH + 1], absc[:, 0:NCH], axis=AX.X),
              reads=(rAbs,), writes=(rAbs,))
        mm(pb[0][0:1, 0:1], onec[:], absc[:, NCH:NCH + 1], 0, True, True,
           (rAbs, rOnes))
        kb.op("vector", lambda E: E.tensor_scalar(
            R[0:1, 0:1], pb[0][0:1, 0:1], float(1.0 / (D * D)), 1e-10,
            ALU.mult, ALU.add), reads=(rB[0],), writes=(rRow,))
        kb.op("vector", lambda E: E.reciprocal(R[0:1, 1:2], R[0:1, 0:1]),
              reads=(rRow,), writes=(rRow,))
        kb.op("vector", lambda E: E.tensor_scalar_add(
            R[0:1, 0:1], R[0:1, 0:1], -1e-10), reads=(rRow,), writes=(rRow,))
        mm(pb[0][:, 0:2], oner[:], R[0:1, 0:2], 0, True, True, (rRow, rOnes))
        kb.op("vector", lambda E: E.tensor_copy(sicol[:], pb[0][:, 0:2]),
              reads=(rB[0],), writes=(rSic,))

    def quant_pass2():
        kb.mark("quant2")
        for rt in range(NCH):
            yield
            slot = 2 + rt % 2
            wsl = WP[:, slot * 2048:(slot + 1) * 2048]
            dma("sync", f"d_s{slot}", wsl, wfn[rt * 128:(rt + 1) * 128, :],
                (), (rWP[slot],))
            for sp in range(4):
                wst = wsl[:, sp * 512:(sp + 1) * 512]
                sA, rA_ = Stile[sp % 2], rS[sp % 2]
                sB, rB_ = Stile[2 + sp % 2], rS[2 + sp % 2]
                kb.op("vector", lambda E, wst=wst, sA=sA: E.tensor_scalar(
                    sA[:], wst, sicol[:, 1:2], 0.5, ALU.mult, ALU.is_gt),
                      reads=(rWP[slot], rSic), writes=(rA_,))
                kb.op("vector", lambda E, wst=wst, sB=sB: E.tensor_scalar(
                    sB[:], wst, sicol[:, 1:2], -0.5, ALU.mult, ALU.is_lt),
                      reads=(rWP[slot], rSic), writes=(rB_,))
                kb.op("vector", lambda E, wst=wst, sA=sA, sB=sB: E.tensor_sub(
                    wst, sA[:], sB[:]), reads=(rA_, rB_), writes=(rWP[slot],))
            for cb in range(4):
                bank = 4 + (cb % 2)
                for c4 in range(4):
                    c = cb * 4 + c4
                    kb.op("tensor", lambda E, wsl=wsl, c=c, c4=c4, bank=bank:
                          E.transpose(pb[bank][:, c4 * 128:(c4 + 1) * 128],
                                      wsl[:, c * 128:(c + 1) * 128], identt[:]),
                          reads=(rWP[slot], rConst), writes=(rB[bank],))
                st4 = Stile[2 + cb % 2]
                rs4 = rS[2 + cb % 2]
                kb.op("vector", lambda E, bank=bank, st4=st4: E.tensor_copy(
                    st4[:], pb[bank][:]), reads=(rB[bank],), writes=(rs4,))
                def wqt_out(cb=cb, rt=rt):
                    return bass.AP(wqt.tensor, (cb * 4 * 128) * D + rt * 128,
                                   [[D, 128], [128 * D, 4], [1, 128]])
                dma("gpsimd", f"d_wqt{cb % 2}", wqt_out(),
                    st4[:].rearrange("p (c j) -> p c j", c=4),
                    (rs4,), (rWqt,))
        flush(["d_wqt0", "d_wqt1"], (rWqt,), clean_res=rS)
        yield

    def c16r(t, c):
        return RTW[id(t)][:, c * 512:(c + 1) * 512]

    def wslot(i):
        return WP[:, i * 2048:(i + 1) * 2048]

    def wslotr(i):
        return WPR[:, i * 2048:(i + 1) * 2048]

    wctr = [0]

    def nslot():
        s = wctr[0] % 4
        wctr[0] += 1
        return s

    for g in range(2):
        W = WIDTH[g]
        vis = VIS[g]
        diag_ci = DIAG_CI[g]
        nt128 = W // 128

        q1gen = quant_pass1() if g == 0 else None
        kb.mark(f"hrec_g{g}")
        dma("gpsimd", "d_r0", rseg(A_), acr[2 * g:2 * g + 1, :],
            (rAcr, rRow), (rRow,))
        dma("gpsimd", "d_r1", rseg(B_), acr[2 * g + 1:2 * g + 2, :],
            (rAcr, rRow), (rRow,))
        broadcast(rseg(A_), rRow, bb0, rBB[0], bank=0)
        broadcast(rseg(B_), rRow, bb1, rBB[1], bank=1)
        for c in range(NCH):
            if q1gen is not None:
                next(q1gen, None)
            sA, rA_ = Stile[c % 2], rS[c % 2]
            sB, rB_ = Stile[2 + c % 2], rS[2 + c % 2]
            dma("gpsimd", f"d_r{c % 2}", sA[:],
                xfm[c * 128:(c + 1) * 128, g * 512:(g + 1) * 512],
                (rXfm[g],), (rA_,))
            kb.op("vector", lambda E, sA=sA, sB=sB: E.tensor_mul(
                sB[:], sA[:], bb0[:]), reads=(rA_, rBB[0]), writes=(rB_,))
            kb.op("vector", lambda E, sB=sB: E.tensor_sub(
                sB[:], sB[:], bb1[:]), reads=(rB_, rBB[1]), writes=(rB_,))
            kb.op("scalar", lambda E, c=c, sB=sB: E.activation(
                c16(T1, c), sB[:], ACT.Identity, bias=b1t[:, c:c + 1],
                scale=g1t[:, c:c + 1]), reads=(rB_, rConst), writes=(rT1[c],))
        if q1gen is not None:
            for _ in q1gen:
                pass

        kb.mark(f"q_g{g}")
        for m in range(NH):
            slot = nslot()
            bank = 2 + m % 2
            dma("sync", f"d_s{slot}", WP[:, slot * 2048:(slot + 1) * 2048],
                colblock(wq, m * 128), (), (rWP[slot],))
            prewait("tensor", rT1 + [rWP[slot]])
            for c in range(NCH):
                mm(pb[bank][:],
                   WP[:, slot * 2048 + c * 128:slot * 2048 + (c + 1) * 128],
                   c16(T1, c), bank, c == 0, c == NCH - 1, (rWP[slot], rT1[c]))
            kb.op("scalar", lambda E, m=m, bank=bank: E.activation(
                c16(T2, m), pb[bank][:], ACT.Identity, bias=bqt[:, m:m + 1]),
                  reads=(rB[bank], rConst), writes=(rT2[m],))

        q2gen = quant_pass2() if g == 0 else None
        kb.mark(f"attn_g{g}")

        def emit_scores(h):
            for st in range(4):
                srow0 = (st % 2) * 2048
                sres = [rT4[(st % 2) * 4 + i] for i in range(4)]
                for ci, ch in enumerate(vis):
                    bank = 4 + ci % 2
                    mm(pb[bank][:],
                       T2[:, h * 512 + st * 128:h * 512 + (st + 1) * 128],
                       kbuf[:, ch * 512:(ch + 1) * 512], bank, True, True,
                       (rT2[h], rK[ch]))
                    if ci == diag_ci:
                        kb.op("vector", lambda E, bank=bank, srow0=srow0,
                              ci=ci, st=st: E.tensor_add(
                                  T4[:, srow0 + ci * 512:srow0 + (ci + 1) * 512],
                                  pb[bank][:],
                                  maskst[:, 384 - 128 * st:896 - 128 * st]),
                              reads=(rB[bank], rConst), writes=(sres[ci],))
                    else:
                        kb.op("scalar", lambda E, bank=bank, srow0=srow0,
                              ci=ci, g=g, ch=ch: E.activation(
                                  T4[:, srow0 + ci * 512:srow0 + (ci + 1) * 512],
                                  pb[bank][:], ACT.Identity,
                                  bias=cmaskt[:, g * 4 + ch:g * 4 + ch + 1]),
                              reads=(rB[bank], rConst), writes=(sres[ci],))
                par2 = st % 2
                prow0 = 4096 + par2 * 2048
                kb.op("vector", lambda E, srow0=srow0, W=W, par2=par2:
                      E.reduce_max(negm[:, par2:par2 + 1],
                                   T4[:, srow0:srow0 + W], axis=AX.X,
                                   negate=True),
                      reads=tuple(sres), writes=(rNegm[par2],))
                pres = [rT4[8 + par2 * 4 + i] for i in range(4)]
                kb.op("scalar", lambda E, srow0=srow0, W=W, par2=par2,
                      prow0=prow0: E.activation(
                          T4[:, prow0:prow0 + W], T4[:, srow0:srow0 + W],
                          ACT.Exp, bias=negm[:, par2:par2 + 1],
                          accum_out=lsum[:, par2:par2 + 1]),
                      reads=tuple(sres) + (rNegm[par2],),
                      writes=tuple(pres) + (rLsum[par2],))
                kb.op("vector", lambda E, par2=par2: E.reciprocal(
                    lrec[:, par2:par2 + 1], lsum[:, par2:par2 + 1]),
                      reads=(rLsum[par2],), writes=(rLrec[par2],))
                kb.op("vector", lambda E, st=st, W=W, par2=par2, prow0=prow0:
                      E.tensor_scalar_mul(
                          T3[:, st * 2048:st * 2048 + W],
                          T4[:, prow0:prow0 + W], lrec[:, par2:par2 + 1]),
                      reads=tuple(pres) + (rLrec[par2],),
                      writes=tuple(rT3[st * 4 + i] for i in range(4)))

        def emit_pv(h):
            pend = None
            for jj in range(nt128):
                true128 = vis[jj // 4] * 4 + (jj % 4)
                par = jj % 2
                ptbank = 6 if par == 0 else 0
                for st in range(4):
                    kb.op("tensor", lambda E, st=st, jj=jj, ptbank=ptbank:
                          E.transpose(
                              pb[ptbank][:, st * 128:(st + 1) * 128],
                              T3[:, st * 2048 + jj * 128:st * 2048 + (jj + 1) * 128],
                              identt[:]),
                          reads=(rT3[st * 4 + jj // 4], rConst),
                          writes=(rB[ptbank],))
                ptt = bb0 if par == 0 else bb1
                ev_eng = "scalar" if par == 0 else "vector"
                if ev_eng == "scalar":
                    kb.op("scalar", lambda E, ptt=ptt, ptbank=ptbank:
                          E.activation(ptt[:], pb[ptbank][:], ACT.Identity),
                          reads=(rB[ptbank],), writes=(rBB[par],))
                else:
                    kb.op("vector", lambda E, ptt=ptt, ptbank=ptbank:
                          E.tensor_copy(ptt[:], pb[ptbank][:]),
                          reads=(rB[ptbank],), writes=(rBB[par],))
                if pend is not None:
                    mm(pb[7][:], pend[0], pend[1], 7, pend[2], pend[3], pend[4])
                pend = (vt[:, true128 * 128:(true128 + 1) * 128], ptt[:],
                        jj == 0, jj == nt128 - 1, (rV[true128 // 4], rBB[par]))
            mm(pb[7][:], pend[0], pend[1], 7, pend[2], pend[3], pend[4])
            kb.op("scalar", lambda E, h=h: E.activation(
                c16(T1, h), pb[7][:], ACT.Identity),
                  reads=(rB[7],), writes=(rT1[h],))

        for h in range(NH):
            if q2gen is not None:
                next(q2gen, None)
            emit_scores(h)
            emit_pv(h)
        if q2gen is not None:
            for _ in q2gen:
                pass

        for c in range(NCH):
            dma("gpsimd", f"d_r{c % 2}", c16(T3, c),
                tfm[c * 128:(c + 1) * 128, g * 512:(g + 1) * 512],
                (rTfm[g],), (rT3[c],))

        kb.mark(f"wo_g{g}")
        for m in range(NH):
            slot = nslot()
            bank = 2 + m % 2
            dma("sync", f"d_s{slot}", WP[:, slot * 2048:(slot + 1) * 2048],
                colblock(wo, m * 128), (), (rWP[slot],))
            prewait("tensor", rT1 + [rWP[slot]])
            for c in range(NCH):
                mm(pb[bank][:],
                   WP[:, slot * 2048 + c * 128:slot * 2048 + (c + 1) * 128],
                   c16(T1, c), bank, c == 0, c == NCH - 1, (rWP[slot], rT1[c]))
            sA, rA_ = Stile[m % 2], rS[m % 2]
            kb.op("scalar", lambda E, m=m, bank=bank, sA=sA: E.activation(
                sA[:], pb[bank][:], ACT.Identity, bias=bot[:, m:m + 1]),
                  reads=(rB[bank], rConst), writes=(rA_,))
            kb.op("vector", lambda E, m=m, sA=sA: E.tensor_add(
                c16(T2, m), sA[:], c16(T3, m)), reads=(rA_, rT3[m]),
                  writes=(rT2[m],))

        kb.mark(f"ln2_g{g}")
        ln_stats(rT2, T2, "scalar")
        broadcast(rseg(C_), rRow, bb0, rBB[0], bank=0)
        broadcast(rseg(B_), rRow, bb1, rBB[1], bank=1)
        for c in range(NCH):
            sA, rA_ = Stile[c % 2], rS[c % 2]
            sB, rB_ = Stile[2 + c % 2], rS[2 + c % 2]
            kb.op("vector", lambda E, c=c, sA=sA: E.tensor_sub(
                sA[:], c16(T2, c), bb0[:]), reads=(rT2[c], rBB[0]), writes=(rA_,))
            kb.op("vector", lambda E, sA=sA, sB=sB: E.tensor_mul(
                sB[:], sA[:], bb1[:]), reads=(rA_, rBB[1]), writes=(rB_,))
            kb.op("scalar", lambda E, c=c, sB=sB: E.activation(
                c16(T1, c), sB[:], ACT.Identity, bias=b2t[:, c:c + 1],
                scale=g2t[:, c:c + 1]), reads=(rB_, rConst), writes=(rT1[c],))

        kb.mark(f"ffn_g{g}")
        for icg in range(4):
            for ici in range(16):
                ic = icg * 16 + ici
                slot = nslot()
                bank = 2 + ici % 2
                dma("sync", f"d_s{slot}", WP[:, slot * 2048:(slot + 1) * 2048],
                    colblock(wfc, ic * 128), (), (rWP[slot],))
                prewait("tensor", rT1 + [rWP[slot]])
                for c in range(NCH):
                    mm(pb[bank][:],
                       WP[:, slot * 2048 + c * 128:slot * 2048 + (c + 1) * 128],
                       c16(T1, c), bank, c == 0, c == NCH - 1,
                       (rWP[slot], rT1[c]))
                kb.op("scalar", lambda E, ic=ic, ici=ici, bank=bank: E.activation(
                    c16(T3, ici), pb[bank][:], ACT.Gelu_apprx_tanh,
                    bias=bfct[:, ic:ic + 1]), reads=(rB[bank], rConst),
                      writes=(rT3[ici],))
            for m in range(NH):
                slot = nslot()
                bank = 4 + m % 2
                wprap = bass.AP(wpr.tensor, (icg * 2048) * D + m * 128,
                                [[D, 128], [128 * D, 16], [1, 128]])
                dma("sync", f"d_s{slot}", WP[:, slot * 2048:(slot + 1) * 2048],
                    wprap, (), (rWP[slot],))
                prewait("tensor", rT3 + [rWP[slot]])
                for k in range(16):
                    mm(pb[bank][:],
                       WP[:, slot * 2048 + k * 128:slot * 2048 + (k + 1) * 128],
                       c16(T3, k), bank, k == 0, k == 15, (rWP[slot], rT3[k]))
                if icg == 0:
                    kb.op("vector", lambda E, m=m, bank=bank: E.tensor_copy(
                        c16(T4, m), pb[bank][:]), reads=(rB[bank],),
                          writes=(rT4[m],))
                else:
                    kb.op("vector", lambda E, m=m, bank=bank: E.tensor_add(
                        c16(T4, m), c16(T4, m), pb[bank][:]),
                          reads=(rB[bank], rT4[m]), writes=(rT4[m],))

        kb.mark(f"x1_g{g}")
        for c in range(NCH):
            sA, rA_ = Stile[c % 2], rS[c % 2]
            sB, rB_ = Stile[2 + c % 2], rS[2 + c % 2]
            dma("gpsimd", f"d_r{c % 2}", sB[:],
                xfm[c * 128:(c + 1) * 128, g * 512:(g + 1) * 512],
                (rXfm[g],), (rB_,))
            kb.op("vector", lambda E, c=c, sA=sA: E.scalar_tensor_tensor(
                sA[:], c16(T4, c), bprt[:, c:c + 1], c16(T2, c),
                ALU.add, ALU.add), reads=(rT4[c], rT2[c], rConst), writes=(rA_,))
            kb.op("vector", lambda E, c=c, sA=sA, sB=sB: E.tensor_add(
                c16(T1, c), sA[:], sB[:]), reads=(rA_, rB_), writes=(rT1[c],))

        kb.mark(f"ln3_g{g}")
        ln_stats(rT1, T1, "scalar")
        broadcast(rseg(C_), rRow, bb0, rBB[0], bank=0)
        broadcast(rseg(B_), rRow, bb1, rBB[1], bank=1)
        for c in range(NCH):
            sA, rA_ = Stile[c % 2], rS[c % 2]
            kb.op("vector", lambda E, c=c, sA=sA: E.tensor_sub(
                sA[:], c16(T1, c), bb0[:]), reads=(rT1[c], rBB[0]), writes=(rA_,))
            kb.op("vector", lambda E, c=c, sA=sA: E.tensor_mul(
                c16(T2, c), sA[:], bb1[:]), reads=(rA_, rBB[1]), writes=(rT2[c],))

        kb.mark(f"fin_g{g}")
        for m in range(NH):
            slot = nslot()
            bank = 2 + m % 2
            dma("sync", f"d_s{slot}", WP[:, slot * 2048:(slot + 1) * 2048],
                colblock(wqt, m * 128), (rWqt,), (rWP[slot],))
            prewait("tensor", rT2 + [rWP[slot]])
            for k in range(16):
                mm(pb[bank][:],
                   WP[:, slot * 2048 + k * 128:slot * 2048 + (k + 1) * 128],
                   c16(T2, k), bank, k == 0, k == 15, (rWP[slot], rT2[k]))
            sA, rA_ = Stile[m % 2], rS[m % 2]
            kb.op("vector", lambda E, bank=bank, sA=sA, m=m: E.tensor_scalar(
                sA[:], pb[bank][:], sicol[:, 0:1], bfnt[:, m:m + 1],
                ALU.mult, ALU.add), reads=(rB[bank], rConst, rSic),
                  writes=(rA_,))
            kb.op("vector", lambda E, m=m, sA=sA: E.tensor_add(
                c16(T3, m), sA[:], c16(T1, m)), reads=(rA_, rT1[m]),
                  writes=(rT3[m],))

        kb.mark(f"out_g{g}")
        for t in range(4):
            par = t % 2
            ost = T4[:, par * 4096:par * 4096 + 2048]
            ores = [rT4[par * 8 + i] for i in range(4)]
            for cb in range(4):
                bank = 4 + cb % 2
                for c4 in range(4):
                    c = cb * 4 + c4
                    kb.op("tensor", lambda E, c=c, t=t, c4=c4, bank=bank:
                          E.transpose(pb[bank][:, c4 * 128:(c4 + 1) * 128],
                                      T3[:, c * 512 + t * 128:
                                         c * 512 + (t + 1) * 128],
                                      identt[:]),
                          reads=(rT3[c], rConst), writes=(rB[bank],))
                kb.op("vector", lambda E, cb=cb, bank=bank, ost=ost:
                      E.tensor_copy(ost[:, cb * 512:(cb + 1) * 512],
                                    pb[bank][:]),
                      reads=(rB[bank],), writes=(ores[cb],))
            dma("gpsimd", f"d_o{par}",
                out[(g * 512 + t * 128):(g * 512 + (t + 1) * 128), :],
                ost, tuple(ores), ())

    for par in range(2):
        tot = kb.cnt.get(f"d_o{par}", 0)
        if tot:
            kb.barrier_wait("gpsimd", f"d_o{par}", tot)

    sems_needed = set(kb.cnt.keys()) | set(kb.engsem.values())
    ctx = kb.realize(sems_needed)
    ctx.close()
    outer.close()
    build_program.marks = kb.marks
    build_program.streams = {e: [i for i in kb.stream[e]] for e in ENGS}
    return nc


_PROG = None
ORDER = {0: [0, 3, 1, 2], 1: [1, 2, 0, 3]}


def _host_inputs(inputs):
    x = np.asarray(inputs["L0_attn_hidden_states"], np.float32)

    def colmaj(v, k):
        return np.ascontiguousarray(np.asarray(v, np.float32).reshape(k, 128).T)

    shared = {
        "wq": np.ascontiguousarray(np.asarray(inputs["L0_attn_wq"], np.float32)),
        "wkv": np.ascontiguousarray(np.asarray(inputs["L0_attn_wkv"], np.float32)),
        "wo": np.ascontiguousarray(np.asarray(inputs["L0_attn_wo"], np.float32)),
        "wfc": np.ascontiguousarray(np.asarray(inputs["L0_attn_wfc"], np.float32)),
        "wpr": np.ascontiguousarray(np.asarray(inputs["L0_attn_wproj"], np.float32)),
        "wfn": np.ascontiguousarray(np.asarray(inputs["L0_ffn_weight"], np.float32)),
        "g1": colmaj(inputs["L0_attn_ln1_g"], NCH),
        "b1": colmaj(inputs["L0_attn_ln1_b"], NCH),
        "g2": colmaj(inputs["L0_attn_ln2_g"], NCH),
        "b2": colmaj(inputs["L0_attn_ln2_b"], NCH),
        "bq": colmaj(inputs["L0_attn_bq"], NCH),
        "bkv": colmaj(inputs["L0_attn_bkv"], 2),
        "bo": colmaj(inputs["L0_attn_bo"], NCH),
        "bfc": colmaj(inputs["L0_attn_bfc"], NIC),
        "bpr": colmaj(inputs["L0_attn_bproj"], NCH),
        "bfn": colmaj(inputs["L0_ffn_bias"], NCH),
        "ident": np.eye(128, dtype=np.float32),
    }
    s_idx = np.arange(128)[:, None]
    u_idx = np.arange(896)[None, :]
    shared["masks"] = np.ascontiguousarray(
        np.where(u_idx - 384 <= s_idx, 0.0, NEG).astype(np.float32))

    in_maps = []
    for core in range(8):
        b, half = core // 2, core % 2
        order = ORDER[half]
        xb = np.ascontiguousarray(
            np.concatenate([x[b, 512 * q:512 * (q + 1)] for q in order], axis=0))
        cm = np.zeros(8, np.float32)
        for g in range(2):
            qg = order[g]
            for ci, ch in enumerate(VIS[g]):
                if ci == DIAG_CI[g]:
                    cm[g * 4 + ch] = 0.0
                else:
                    cm[g * 4 + ch] = 0.0 if order[ch] < qg else NEG
        cmask = np.ascontiguousarray(
            np.broadcast_to(cm[None, :], (128, 8))).astype(np.float32)
        m = {"x": xb, "cmask": cmask}
        m.update(shared)
        in_maps.append(m)
    return in_maps


def kernel(**inputs):
    global _PROG
    in_maps = _host_inputs(inputs)
    if _PROG is None:
        _PROG = build_program()
    res = run_bass_kernel_spmd(_PROG, in_maps, core_ids=list(range(8)))
    outx = np.empty((4, 2048, D), np.float32)
    for core in range(8):
        b, half = core // 2, core % 2
        order = ORDER[half]
        o = np.asarray(res.results[core]["out"])
        outx[b, 512 * order[0]:512 * (order[0] + 1)] = o[0:512]
        outx[b, 512 * order[1]:512 * (order[1] + 1)] = o[512:1024]
    return outx
